# version 06 (final)
# baseline (speedup 1.0000x reference)
import numpy as np
import ml_dtypes
from contextlib import ExitStack

import concourse.bass as bass
import concourse.mybir as mybir
from concourse.bass_utils import run_bass_kernel_spmd

F32 = mybir.dt.float32
BF16 = mybir.dt.bfloat16
I32 = mybir.dt.int32
AF = mybir.ActivationFunctionType
ALU = mybir.AluOpType
AX = mybir.AxisListType

B, T, D = 16, 4096, 1024
H, DK, DV = 4, 256, 512
INNER = 4096
NCORES = 8
SEQ_PER_CORE = B // NCORES
C = 128
NCH = T // C
TOK = SEQ_PER_CORE * T
EPS = 1e-5
ENGINES = ("sync", "scalar", "tensor", "vector", "gpsimd")

_TAPS = {}
_NCH_OVERRIDE = None
_MAX_PHASE = 9
_NO_WQJOBS = False
_WQ_SUBSET = "all"
_DMA_STEP = 16


class Em:

    def __init__(self, nc):
        self.nc = nc
        self.thunks = {e: [] for e in ENGINES}
        self.sems = {}
        self.cnt = {}
        self.hw = {e: {} for e in ENGINES}
        self.dma_sems = set()

    def new_sem(self, name, es, dma=False):
        self.sems[name] = es.enter_context(self.nc.semaphore(name))
        self.cnt[name] = 0
        if dma:
            self.dma_sems.add(name)

    def op(self, eng, fn, waits=(), inc=None):
        wl = []
        for w in waits:
            if w is None:
                continue
            s, v = w
            assert v <= self.cnt[s], f"[{eng}] wait on future {s}:{v}>{self.cnt[s]}"
            if v > self.hw[eng].get(s, 0):
                self.hw[eng][s] = v
                wl.append((s, v))
        tok = None
        if inc is not None:
            self.cnt[inc] += _DMA_STEP if inc in self.dma_sems else 1
            tok = (inc, self.cnt[inc])
        self.thunks[eng].append((fn, tuple(wl), inc))
        return tok

    def dma(self, eng, sem_name, out, in_, waits=(), slow=False):
        return self.op(eng, (lambda h, o=out, i=in_, s=slow: h.dma_start(
            o, i, allow_slow_non_contiguous=s)), waits=waits, inc=sem_name)

    def opd(self, eng, fn, waits=(), inc=None):
        tok = self.op(eng, fn, waits=waits, inc=inc)
        self.op(eng, lambda h: h.drain())
        return tok

    def wait(self, eng, waits):
        self.op(eng, None, waits=waits)

    def quiesce(self):
        self.wait("sync", [("lds", self.cnt["lds"])])
        self.wait("scalar", [("sts", self.cnt["sts"])])
        self.wait("gpsimd", [("ldg", self.cnt["ldg"])])

    def replay_block(self):
        thunks, self.thunks = self.thunks, {e: [] for e in ENGINES}

        with self.nc.Block() as blk:
            def mk(eng):
                def body(h):
                    for fn, wl, inc in thunks[eng]:
                        for s, v in wl:
                            h.wait_ge(self.sems[s], v)
                        if fn is None:
                            continue
                        r = fn(h)
                        if inc is not None:
                            step = _DMA_STEP if inc in self.dma_sems else 1
                            r.then_inc(self.sems[inc], step)
                return body
            if thunks["sync"]:
                blk.sync(mk("sync"))
            if thunks["scalar"]:
                blk.scalar(mk("scalar"))
            if thunks["tensor"]:
                blk.tensor(mk("tensor"))
            if thunks["vector"]:
                blk.vector(mk("vector"))
            if thunks["gpsimd"]:
                blk.gpsimd(mk("gpsimd"))


def _heron(em, ve_ap, s_ap, r_ap, out_ap, iters, seed_mul, seed_add, waits=()):
    em.opd("vector", lambda h: h.tensor_scalar(
        s_ap, ve_ap, float(seed_mul), float(seed_add), ALU.mult, ALU.add),
        waits=waits)
    for _ in range(iters):
        em.opd("vector", lambda h: h.reciprocal(r_ap, s_ap))
        em.opd("vector", lambda h: h.tensor_tensor(r_ap, ve_ap, r_ap, ALU.mult))
        em.opd("vector", lambda h: h.tensor_tensor(s_ap, s_ap, r_ap, ALU.add))
        em.opd("vector", lambda h: h.tensor_single_scalar(s_ap, s_ap, 0.5,
                                                          ALU.mult))
    return em.opd("vector", lambda h: h.reciprocal(out_ap, s_ap), inc="dv")


def _bcast_tiles(em, nc, mod_hbm, modt, oneh, ps_bf, ad, spec, t_oneh,
                 ps_free=None):
    t_z = em.op("gpsimd", lambda h: h.memset(modt[:, :], 0.0), inc="gp")
    t_m = em.dma("gpsimd", "ldg", modt[0:SEQ_PER_CORE, :], mod_hbm[:, :],
                 waits=[("sts", em.cnt["sts"])])
    t_ev = None
    for i, b, slot in spec:
        t_pe = None
        for ns in range(2):
            t_pe = em.op("tensor", lambda h, i=i, b=b, slot=slot, ns=ns:
                         h.matmul(ps_bf[:, ns * 512:(ns + 1) * 512],
                                  oneh[:, b * 128:(b + 1) * 128],
                                  modt[0:B, slot * D + ns * 512:
                                       slot * D + ns * 512 + 512],
                                  start=True, stop=True),
                         waits=[t_m, t_z, t_oneh, t_ev, ps_free],
                         inc="pe" if ns == 1 else None)
        t_ev = em.op("scalar", lambda h, i=i: h.copy(ad[i][:, :],
                     ps_bf[:, 0:1024]), waits=[t_pe], inc="ac")
    return t_ev


def build_program():
    nch = _NCH_OVERRIDE or NCH
    ncht = SEQ_PER_CORE * nch
    nc = bass.Bass(target_bir_lowering=False, trn_type="TRN2")

    def dr(n, s, d, k=None):
        return nc.dram_tensor(n, s, d, kind=k) if k else nc.dram_tensor(n, s, d)

    x_h = dr("x", [TOK, D], F32, "ExternalInput")
    wq_h = dr("wq_t", [D, D], F32, "ExternalInput")
    wk_h = dr("wk_t", [D, D], F32, "ExternalInput")
    wv_h = dr("wv_t", [D, 2 * D], F32, "ExternalInput")
    wg_h = dr("wg_t", [D, 2 * D], F32, "ExternalInput")
    wo_h = dr("wo_t", [2 * D, D], F32, "ExternalInput")
    wmod_h = dr("wmod", [D, 6 * D], F32, "ExternalInput")
    wgate_h = dr("wgate_t", [D, INNER], F32, "ExternalInput")
    wup_h = dr("wup_t", [D, INNER], F32, "ExternalInput")
    wdown_h = dr("wdown_t", [INNER, D], F32, "ExternalInput")
    ct_h = dr("c_t", [D, 128], F32, "ExternalInput")
    gnw_h = dr("gnw_pack", [128, 16], F32, "ExternalInput")
    ident_h = dr("ident", [128, 128], BF16, "ExternalInput")
    identf_h = dr("ident_f32", [128, 128], F32, "ExternalInput")
    cos_h = dr("rope_cos", [T, 128], F32, "ExternalInput")
    sin_h = dr("rope_sin", [T, 128], F32, "ExternalInput")
    maskt_h = dr("maskt_pack", [128, 4 * 128], F32, "ExternalInput")
    cross_h = dr("cross_pack", [128, 4 * 128], F32, "ExternalInput")
    kdec_h = dr("kdec_pack", [128, H], F32, "ExternalInput")
    oneh_h = dr("onehot", [B, 2 * 128], BF16, "ExternalInput")

    y_h = dr("y", [TOK, D], F32, "ExternalOutput")

    mod_hbm = dr("mod_scr", [SEQ_PER_CORE, 6 * D], F32)
    zt_hbm = dr("zt_scr", [ncht, 128, 2 * D], BF16)
    x1_hbm = dr("x1_scr", [TOK, D], F32)
    h2qt_hbm = dr("h2qt_scr", [ncht, 128, D], BF16)
    uq_hbm = dr("uq_scr", [ncht, 128, INNER], BF16)
    wqg_hbm = dr("wqg_scr", [D, INNER], BF16)
    wqu_hbm = dr("wqu_scr", [D, INNER], BF16)
    wqd_hbm = dr("wqd_scr", [INNER, D], BF16)
    wsc_hbm = dr("wsc_scr", [3, 2], F32)

    tap_specs = {
        "mod": ([SEQ_PER_CORE, 6 * D], F32), "h": ([128, D], BF16), "q": ([128, D], BF16),
        "k": ([128, D], BF16), "v": ([128, 2 * D], BF16),
        "sig": ([128, 2 * D], BF16), "o": ([128, 2 * D], BF16),
        "z": ([128, 2 * D], BF16), "S": ([128, 8 * DV], F32),
        "x1": ([128, D], F32), "h2q": ([128, D], BF16),
        "u": ([128, INNER], BF16), "uq": ([128, INNER], BF16),
        "xn": ([128, D], BF16), "stats": ([128, 40], F32),
        "wsum": ([128, 16], F32),
        "wsum2": ([128, 8], F32),
        "modv": ([SEQ_PER_CORE, 6 * D], F32),
        "adA": ([128, 4 * D], BF16), "adB": ([128, 6 * D], BF16),
        "adD": ([128, 2 * D], BF16), "modb2": ([B, 6 * D], BF16),
        "scb": ([128, 6], F32), "hT": ([128, D], BF16),
        "qT": ([128, D], BF16), "sct": ([128, 128], BF16),
    }
    taps = {n: dr("tap_" + n, *tap_specs[n], "ExternalOutput") for n in _TAPS}

    gammas = [1.0 - 2.0 ** (-5.0 - hh) for hh in range(H)]
    gCs = [float(np.float32(np.float64(g) ** C)) for g in gammas]

    es = ExitStack()
    em = Em(nc)
    with es:
        for s_ in ("lds", "ldg", "sts"):
            em.new_sem(s_, es, dma=True)
        for s_ in ("pe", "dv", "ac", "gp"):
            em.new_sem(s_, es)

        for eng, sems in (("sync", ("lds",)), ("gpsimd", ("ldg", "gp")),
                          ("scalar", ("sts", "ac")), ("tensor", ("pe",)),
                          ("vector", ("dv",))):
            for s_ in sems:
                em.op(eng, lambda h, ss=s_: h.sem_clear(em.sems[ss]))
        em.replay_block()

        sbp = lambda n, s, d=F32: es.enter_context(nc.sbuf_tensor(n, s, d))
        ident = sbp("sb_ident", [128, 128], BF16)
        identf = sbp("sb_identf", [128, 128])
        maskt = sbp("sb_maskt", [128, 4 * 128])
        cross = sbp("sb_cross", [128, 4 * 128])
        kdec = sbp("sb_kdec", [128, H])
        gnw = sbp("sb_gnw", [128, 16])
        stats = sbp("sb_stats", [128, 40])
        hersc = sbp("sb_hersc", [128, 8])
        trash = sbp("sb_trash", [128, D], BF16)
        scb = sbp("sb_scb", [128, 6])
        ad = [sbp(f"sb_ad{i}", [128, D], BF16) for i in range(6)]
        oneh = sbp("sb_oneh", [B, 2 * 128], BF16)

        t_ident = em.dma("sync", "lds", ident[:, :], ident_h[:, :])
        t_idf = em.dma("sync", "lds", identf[:, :], identf_h[:, :])
        em.dma("sync", "lds", maskt[:, :], maskt_h[:, :])
        t_cross = em.dma("sync", "lds", cross[:, :], cross_h[:, :])
        em.dma("sync", "lds", kdec[:, :], kdec_h[:, :])
        t_gnw = em.dma("sync", "lds", gnw[:, :], gnw_h[:, :])
        t_oneh = em.dma("sync", "lds", oneh[:, :], oneh_h[:, :])

        aw = ExitStack()
        sbw = lambda n, s, d: aw.enter_context(nc.sbuf_tensor(n, s, d))
        wq_sb = sbw("sb_wq", [128, 8 * D], BF16)
        wk_sb = sbw("sb_wk", [128, 8 * D], BF16)
        wv_sb = sbw("sb_wv", [128, 8 * 2 * D], BF16)
        wg_sb = sbw("sb_wg", [128, 8 * 2 * D], BF16)

        def wload_t(dst, src_ap, dout, nic=8):
            sap = bass.AP(src_ap, 0, [[dout, 128], [128 * dout, nic], [1, dout]])
            dap = bass.AP(dst, 0, [[nic * dout, 128], [dout, nic], [1, dout]])
            return em.dma("gpsimd", "ldg", dap, sap)

        wload_t(wq_sb, wq_h, D)
        wload_t(wk_sb, wk_h, D)
        wload_t(wv_sb, wv_h, 2 * D)
        t_wattn = wload_t(wg_sb, wg_h, 2 * D)

        with ExitStack() as p0:
            sb0 = lambda n, s, d=F32: p0.enter_context(nc.sbuf_tensor(n, s, d))
            SQ = 128
            ct_sb = sb0("p0_ct", [128, 8 * SQ])
            ct_si = sb0("p0_ctsi", [128, 8 * SQ])
            QW = D
            wmb = sb0("p0_wmod", [128, 2 * QW])
            mod_sb = sb0("p0_mod", [SQ, 6 * D])
            tm1 = sb0("p0_tm1", [SQ, D])
            tm2 = sb0("p0_tm2", [SQ, D])
            assert 8 * SQ <= D
            ps_mod = p0.enter_context(nc.psum_tensor("p0_psmod", [SQ, 3 * D], F32))

            sap = bass.AP(ct_h, 0, [[SQ, 128], [128 * SQ, 8], [1, SQ]])
            dap = bass.AP(ct_sb, 0, [[8 * SQ, 128], [SQ, 8], [1, SQ]])
            t_ct = em.dma("sync", "lds", dap, sap)
            t_si = em.op("scalar", lambda h: h.activation(
                ct_si[:, :], ct_sb[:, :], AF.Silu), waits=[t_ct], inc="ac")

            t_pe_ic = {}
            t_halfev = {}
            NQ = 6 * D // QW
            for q in range(NQ):
                half = q // 3
                t_pe = None
                for ic in range(8):
                    par = ic % 2
                    wsl = wmb[:, par * QW:(par + 1) * QW]
                    prev = t_pe_ic.get((q, ic - 2)) or t_pe_ic.get((q - 1, ic + 6))
                    t_dma = em.dma("sync", "lds", wsl,
                                   wmod_h[ic * 128:(ic + 1) * 128,
                                          q * QW:(q + 1) * QW],
                                   waits=[prev])
                    w_ev = ([t_halfev[half - 1]] if (half and q % 3 == 0 and
                                                     ic == 0) else [])
                    for ns in range(2):
                        psc = (q % 3) * QW + ns * 512
                        t_pe = em.op(
                            "tensor",
                            lambda h, o=ps_mod[0:SQ, psc:psc + 512],
                            l=ct_si[:, ic * SQ:(ic + 1) * SQ],
                            r=wsl[:, ns * 512:(ns + 1) * 512],
                            st=(ic == 0), sp=(ic == 7):
                            h.matmul(o, l, r, start=st, stop=sp),
                            waits=[t_dma, t_si] + w_ev,
                            inc="pe" if ns == 1 else None)
                    t_pe_ic[(q, ic)] = t_pe
                if q % 3 == 2:
                    t_halfev[half] = em.op(
                        "vector",
                        lambda h, o=mod_sb[0:SQ, half * 3 * D:(half + 1) * 3 * D]:
                        h.tensor_copy(o, ps_mod[0:SQ, :]),
                        waits=[t_pe, t_pe_ic[(q - 2, 7)], t_pe_ic[(q - 1, 7)]],
                        inc="dv")

            t_modv = t_halfev[1]
            if "modv" in taps:
                em.dma("scalar", "sts", taps["modv"][:, :], mod_sb[0:SEQ_PER_CORE, :],
                       waits=[t_halfev[0], t_halfev[1]])
            t_tr = []
            for off in (D, 4 * D):
                t_tr.append(em.op("vector", lambda h, o=mod_sb[0:SQ, off:off + D]:
                            h.tensor_single_scalar(o, o, 1.0, ALU.add),
                            waits=[t_modv], inc="dv"))
            for off in (2 * D, 5 * D):
                sl = mod_sb[0:SQ, off:off + D]
                em.op("vector", lambda h, s=sl: h.tensor_single_scalar(
                    tm1[0:SQ, :], s, 0.0, ALU.is_ge), waits=[t_modv])
                em.op("vector", lambda h: h.tensor_scalar(
                    tm1[0:SQ, :], tm1[0:SQ, :], 2.0, -1.0, ALU.mult, ALU.add))
                em.op("vector", lambda h, s=sl: h.tensor_single_scalar(
                    tm2[0:SQ, :], s, -1.0, ALU.mult))
                em.op("vector", lambda h, s=sl: h.tensor_tensor(
                    tm2[0:SQ, :], tm2[0:SQ, :], s, ALU.max))
                em.op("vector", lambda h: h.tensor_single_scalar(
                    tm2[0:SQ, :], tm2[0:SQ, :], 0.25, ALU.max))
                t_tr.append(em.op("vector", lambda h, s=sl: h.tensor_tensor(
                    s, tm1[0:SQ, :], tm2[0:SQ, :], ALU.mult), inc="dv"))
            t_modst = em.dma("scalar", "sts", mod_hbm[:, :],
                             mod_sb[0:SEQ_PER_CORE, :], waits=t_tr)
            if "mod" in taps:
                em.dma("scalar", "sts", taps["mod"][:, :],
                       mod_sb[0:SEQ_PER_CORE, :], waits=t_tr)

            em.quiesce()
            em.replay_block()
        if _MAX_PHASE < 1:
            aw.close()
            return nc

        with ExitStack() as pq:
            sbq = lambda n, s, d=F32: pq.enter_context(nc.sbuf_tensor(n, s, d))
            psq = lambda n, s, d=F32: pq.enter_context(nc.psum_tensor(n, s, d))
            land = sbq("q_land", [128, 2048])
            wqi = sbq("q_wqi", [128, 2048], I32)
            tern = sbq("q_tern", [128, 2048], BF16)
            wqst = sbq("q_wqst", [128, 2048], BF16)
            ps_q7 = psq("q_psq7", [128, 512])
            wjobs = []
            wq_state = {"land_free": None, "wqst_free": None}

            def fin_job(wi, count):
                def job():
                    accC = stats[:, 33 + wi:34 + wi]
                    t_pe = em.op("tensor", lambda h, a=accC: h.transpose(
                        ps_q7[0:1, 0:128], a, identf[:, :]),
                        waits=[wq_state["land_free"], t_idf], inc="pe")
                    em.opd("vector", lambda h: h.reduce_sum(
                        stats[0:1, 31:32], ps_q7[0:1, 0:128], AX.X),
                        waits=[t_pe])
                    em.opd("vector", lambda h: h.tensor_scalar(
                        stats[0:1, 31:32], stats[0:1, 31:32],
                        1.0 / count, 1e-5, ALU.mult, ALU.max))
                    t_r = em.opd("vector", lambda h: h.reciprocal(
                        stats[0:1, 32:33], stats[0:1, 31:32]), inc="dv")
                    t_s = em.dma("scalar", "sts", wsc_hbm[wi:wi + 1, :],
                                 stats[0:1, 31:33], waits=[t_r])
                    t_bc = em.dma("gpsimd", "ldg", scb[:, 2 * wi:2 * wi + 2],
                                  bass.AP(wsc_hbm, wi * 2, [[0, 128], [1, 2]]),
                                  waits=[t_s])
                    wq_state[("fin", wi)] = t_bc
                return job

            srcs = [(wgate_h, wqg_hbm, INNER, 0), (wup_h, wqu_hbm, INNER, 1),
                    (wdown_h, wqd_hbm, D, 2)]
            for src, dst, ncol, wi in srcs:
                nrows = src.shape[0]
                first = True
                for r0 in range(0, nrows, 128):
                    for c0 in range(0, ncol, 2048):
                        def mp(src=src, r0=r0, c0=c0, n=min(2048, ncol - c0),
                               wi=wi, first=first):
                            def job():
                                t_d = em.dma("gpsimd", "ldg", land[:, 0:n],
                                             src[r0:r0 + 128, c0:c0 + n],
                                             waits=[wq_state["land_free"]])
                                em.opd("vector", lambda h, n=n: h.reduce_sum(
                                    stats[:, 30:31], land[:, 0:n], AX.X,
                                    apply_absolute_value=True), waits=[t_d])
                                accC = stats[:, 33 + wi:34 + wi]
                                if first:
                                    t = em.opd("vector", lambda h, a=accC:
                                               h.tensor_copy(a, stats[:, 30:31]),
                                               inc="dv")
                                else:
                                    t = em.opd("vector", lambda h, a=accC:
                                               h.tensor_tensor(a, a, stats[:, 30:31],
                                                               ALU.add), inc="dv")
                                wq_state["land_free"] = t
                            return job
                        wjobs.append(mp())
                        first = False
                if _WQ_SUBSET in ("fin", "all", "qnostore"):
                    wjobs.append(fin_job(wi, nrows * ncol))
            for src, dst, ncol, wi in srcs:
                nrows = src.shape[0]
                for r0 in range(0, nrows, 128):
                    for c0 in range(0, ncol, 2048):
                        def qp(src=src, dst=dst, r0=r0, c0=c0,
                               n=min(2048, ncol - c0), wi=wi):
                            def job():
                                t_d = em.dma("gpsimd", "ldg", land[:, 0:n],
                                             src[r0:r0 + 128, c0:c0 + n],
                                             waits=[wq_state["land_free"],
                                                    wq_state[("fin", wi)]])
                                em.op("vector", lambda h, n=n: h.tensor_scalar(
                                    wqi[:, 0:n], land[:, 0:n],
                                    scb[:, 2 * wi + 1:2 * wi + 2], None,
                                    ALU.mult), waits=[t_d])
                                t_free = em.op("vector", lambda h, n=n:
                                               h.tensor_scalar(
                                                   tern[:, 0:n], wqi[:, 0:n],
                                                   1.0, -1.0, ALU.min, ALU.max),
                                               inc="dv")
                                wq_state["land_free"] = t_free
                                t_q = em.op("vector", lambda h, n=n:
                                            h.tensor_scalar(
                                                wqst[:, 0:n], tern[:, 0:n],
                                                scb[:, 2 * wi:2 * wi + 1],
                                                None, ALU.mult),
                                            waits=[wq_state["wqst_free"]],
                                            inc="dv")
                                if _WQ_SUBSET == "qnostore":
                                    wq_state["wqst_free"] = t_q
                                    return
                                wq_state["wqst_free"] = em.dma(
                                    "scalar", "sts", dst[r0:r0 + 128, c0:c0 + n],
                                    wqst[:, 0:n], waits=[t_q])
                            return job
                        if _WQ_SUBSET in ("all", "qnostore"):
                            wjobs.append(qp())
            if _NO_WQJOBS:
                wjobs = []
            jobs_per_chunk = max(1, (len(wjobs) + ncht - 3) // max(1, ncht - 2))

            for j in wjobs:
                j()
            em.quiesce()
            em.replay_block()

        with ExitStack() as pa:
            sba = lambda n, s, d=F32: pa.enter_context(nc.sbuf_tensor(n, s, d))
            psa = lambda n, s, d=F32: pa.enter_context(nc.psum_tensor(n, s, d))
            xb = sba("a_x", [128, 2 * D])
            rb = sba("a_rope", [128, 2 * 256])
            xn = sba("a_xn", [128, D], BF16)
            hbf = sba("a_h", [128, D], BF16)
            hT = sba("a_hT", [128, D], BF16)
            q_sb = sba("a_q", [128, D], BF16)
            k_sb = sba("a_k", [128, D], BF16)
            q_ro = sba("a_qro", [128, D], BF16)
            k_ro = sba("a_kro", [128, D], BF16)
            qT = sba("a_qT", [128, D], BF16)
            kT = sba("a_kT", [128, D], BF16)
            qdT = sba("a_qdT", [128, D], BF16)
            kd_sb = sba("a_kd", [128, D], BF16)
            v_bf = sba("a_v", [128, 2 * D], BF16)
            sg_bf = sba("a_sg", [128, 2 * D], BF16)
            sct = sba("a_sct", [128, 128], BF16)
            o_sb = sba("a_o", [128, 2 * D], BF16)
            S_sb = sba("a_S", [128, 8 * DV])
            S_bf = sba("a_Sbf", [128, 8 * DV], BF16)
            z_sb = sba("a_z", [128, 2 * D], BF16)
            zT_sb = sba("a_zT", [128, 2 * D], BF16)
            rsc = sba("a_rsc", [128, 2 * 128])
            rscg = sba("a_rscg", [128, 2 * 128])
            modt = sba("a_modt", [B, 6 * D], BF16)
            ps_proj = psa("a_psproj", [128, 2048])
            ps_sc = psa("a_pssc", [128, 512])
            ps_o = psa("a_pso", [128, 512])
            ps_tr = psa("a_pstr", [128, 1024], BF16)

            if "wsum" in taps:
                t_ws = None
                tabs = (wq_sb, wk_sb, wv_sb, wg_sb, ident, maskt, cross, kdec,
                        gnw, oneh)
                for wi_, wsb_ in enumerate(tabs):
                    np_ = wsb_.shape[0]
                    t_ws = em.op("vector", lambda h, wi_=wi_, wsb_=wsb_, np_=np_:
                                 h.reduce_sum(stats[0:np_, 24 + wi_:25 + wi_],
                                              wsb_[0:np_, :], AX.X,
                                              apply_absolute_value=True),
                                 waits=[t_wattn, t_oneh], inc="dv")
                em.dma("scalar", "sts", taps["wsum"][:, 0:10], stats[:, 24:34],
                       waits=[t_ws])

            t_bcA = _bcast_tiles(em, nc, mod_hbm, modt, oneh, ps_proj[:, 0:1024],
                                 ad, [(0, 0, 0), (1, 1, 0), (2, 0, 1),
                                      (3, 1, 1)], t_oneh)
            if "adA" in taps:
                for i_ in range(4):
                    em.dma("scalar", "sts", taps["adA"][:, i_ * D:(i_ + 1) * D],
                           ad[i_][:, :], waits=[t_bcA])
            if "wsum" in taps:
                t_ws = None
                for i_ in range(4):
                    t_ws = em.op("vector", lambda h, i_=i_: h.reduce_sum(
                        stats[:, 34 + i_:35 + i_], ad[i_][:, :], AX.X,
                        apply_absolute_value=True), waits=[t_bcA], inc="dv")
                em.dma("scalar", "sts", taps["wsum"][:, 10:14], stats[:, 34:38],
                       waits=[t_ws], slow=True)

            t_x = {}
            t_rp = {}
            t_free_x = {-2: None, -1: None}
            t_free_rope = {-2: (None, None), -1: (None, None)}
            t_st_z = {}
            pair_tok = [t_bcA, t_bcA]
            t_sg = None
            t_bank45 = None
            t_ptr_free = [None]
            t_sbf_tok = [None]
            t_o_free = [None]
            t_zT_prev = [None]
            prev = {"t_xn": None, "t_h": None, "t_qT": None, "t_kT": None,
                    "t_kro_dv": None, "t_z": None, "t_gn": None,
                    "t_gnr": None, "t_upd_pe": None, "t_oev": None,
                    "t_qdT": None, "t_kd": None, "t_sg": None,
                    "t_stats": None}

            def prefetch(cs):
                if cs >= ncht:
                    return
                s, c = divmod(cs, nch)
                row = cs * 128
                t_x[cs] = em.dma("sync", "lds",
                                 xb[:, (cs % 2) * D:(cs % 2 + 1) * D],
                                 x_h[row:row + 128, :],
                                 waits=[t_free_x[cs - 2]])
                par = (cs % 2) * 256
                em.dma("sync", "lds", rb[:, par:par + 128],
                       cos_h[c * 128:(c + 1) * 128, :],
                       waits=list(t_free_rope[cs - 2]))
                t_rp[cs] = em.dma("sync", "lds", rb[:, par + 128:par + 256],
                                  sin_h[c * 128:(c + 1) * 128, :])

            prefetch(0)
            prefetch(1)

            for cs in range(ncht):
                s, c = divmod(cs, nch)
                xsl = xb[:, (cs % 2) * D:(cs % 2 + 1) * D]
                rsl = rb[:, (cs % 2) * 256:(cs % 2) * 256 + 256]
                if c == 0:
                    em.op("gpsimd", lambda h: h.memset(S_sb[:, :], 0.0),
                          waits=[t for t in (t_sbf_tok[0], pair_tok[0],
                                             pair_tok[1]) if t])
                    t_ms2 = em.op("gpsimd", lambda h: h.memset(S_bf[:, :], 0.0),
                                  inc="gp")
                    t_sbf_tok[0] = t_ms2

                em.op("scalar", lambda h, x=xsl: h.activation(
                    trash[:, :], x, AF.Identity, accum_out=stats[:, 0:1]),
                    waits=[t_x[cs], prev["t_stats"]])
                t_ssq = em.op("scalar", lambda h, x=xsl: h.activation(
                    trash[:, :], x, AF.Square, accum_out=stats[:, 1:2]),
                    inc="ac")
                em.opd("vector", lambda h: h.tensor_single_scalar(
                    stats[:, 2:3], stats[:, 0:1], 1.0 / D, ALU.mult),
                    waits=[t_ssq, prev["t_xn"]])
                em.opd("vector", lambda h: h.tensor_single_scalar(
                    stats[:, 3:4], stats[:, 2:3], -1.0, ALU.mult))
                em.opd("vector", lambda h: h.tensor_tensor(
                    stats[:, 4:5], stats[:, 2:3], stats[:, 2:3], ALU.mult))
                em.opd("vector", lambda h: h.tensor_scalar(
                    hersc[:, 0:1], stats[:, 1:2], 1.0 / D, EPS, ALU.mult,
                    ALU.add))
                em.opd("vector", lambda h: h.tensor_tensor(
                    hersc[:, 0:1], hersc[:, 0:1], stats[:, 4:5], ALU.subtract))
                _heron(em, hersc[:, 0:1], hersc[:, 1:2], hersc[:, 2:3],
                       stats[:, 5:6], 12, 0.5, 0.5)
                t_stats = em.opd("vector", lambda h: h.tensor_tensor(
                    stats[:, 6:7], stats[:, 3:4], stats[:, 5:6], ALU.mult),
                    inc="dv")
                prev["t_stats"] = t_stats
                t_xn = em.op("scalar", lambda h, x=xsl: h.activation(
                    xn[:, :], x, AF.Identity, bias=stats[:, 6:7],
                    scale=stats[:, 5:6]),
                    waits=[t_stats, prev["t_h"]], inc="ac")
                t_free_x[cs] = t_xn
                prev["t_xn"] = t_xn
                if _TAPS.get("xn") == cs:
                    em.dma("scalar", "sts", taps["xn"][:, :], xn[:, :],
                           waits=[t_xn])
                if _TAPS.get("stats") == cs:
                    em.dma("scalar", "sts", taps["stats"][:, :], stats[:, :],
                           waits=[t_stats])
                em.op("vector", lambda h, a=ad[2 + s]: h.tensor_tensor(
                    hbf[:, :], xn[:, :], a[:, :], ALU.mult),
                    waits=[t_xn, t_bcA])
                t_h = em.op("vector", lambda h, a=ad[0 + s]: h.tensor_tensor(
                    hbf[:, :], hbf[:, :], a[:, :], ALU.add), inc="dv")
                prev["t_h"] = t_h
                if _TAPS.get("h") == cs:
                    em.dma("scalar", "sts", taps["h"][:, :], hbf[:, :],
                           waits=[t_h])

                for j in range(8):
                    t_pe = em.op("tensor", lambda h, j=j: h.transpose(
                        ps_tr[:, j * 128:(j + 1) * 128],
                        hbf[:, j * 128:(j + 1) * 128], ident[:, :]),
                        waits=[t_h, t_ptr_free[0], t_ident],
                        inc="pe" if j == 7 else None)
                t_hT = em.op("scalar", lambda h: h.copy(hT[:, :], ps_tr[:, :]),
                             waits=[t_pe], inc="ac")
                t_ptr_free[0] = t_hT
                if _TAPS.get("hT") == cs:
                    em.dma("scalar", "sts", taps["hT"][:, :], hT[:, :],
                           waits=[t_hT])

                t_qk = None
                for ic in range(8):
                    lh = hT[:, ic * 128:(ic + 1) * 128]
                    for ni, (wsb, base) in enumerate(((wq_sb, 0), (wk_sb, 1024))):
                        for nsl in range(2):
                            last = (ic == 7 and ni == 1 and nsl == 1)
                            t_qk = em.op(
                                "tensor",
                                lambda h, l=lh,
                                o=ps_proj[:, base + nsl * 512:base + nsl * 512 + 512],
                                r=wsb[:, ic * D + nsl * 512:ic * D + nsl * 512 + 512],
                                st=(ic == 0), sp=(ic == 7):
                                h.matmul(o, l, r, start=st, stop=sp),
                                waits=[t_hT, t_wattn, pair_tok[0], pair_tok[1]],
                                inc="pe" if last else None)
                em.op("vector", lambda h: h.tensor_copy(
                    q_sb[:, :], ps_proj[:, 0:1024]),
                    waits=[t_qk, prev["t_kro_dv"], prev["t_qT"]])
                t_qkev = em.op("vector", lambda h: h.tensor_copy(
                    k_sb[:, :], ps_proj[:, 1024:2048]), inc="dv")

                t_v = None
                for ic in range(8):
                    lh = hT[:, ic * 128:(ic + 1) * 128]
                    for nsl in range(4):
                        last = (ic == 7 and nsl == 3)
                        t_v = em.op(
                            "tensor",
                            lambda h, l=lh,
                            o=ps_proj[:, nsl * 512:(nsl + 1) * 512],
                            r=wv_sb[:, ic * 2 * D + nsl * 512:ic * 2 * D + nsl * 512 + 512],
                            st=(ic == 0), sp=(ic == 7):
                            h.matmul(o, l, r, start=st, stop=sp),
                            waits=[t_qkev], inc="pe" if last else None)
                t_vev = em.op("scalar", lambda h: h.copy(
                    v_bf[:, :], ps_proj[:, :]), waits=[t_v], inc="ac")
                if _TAPS.get("v") == cs:
                    em.dma("scalar", "sts", taps["v"][:, :], v_bf[:, :],
                           waits=[t_vev])

                def rope(eng, src, dst, scr, tok_w, inc):
                    t_last = None
                    for hh in range(H):
                        x1 = src[:, hh * 256:hh * 256 + 128]
                        x2 = src[:, hh * 256 + 128:hh * 256 + 256]
                        o1 = dst[:, hh * 256:hh * 256 + 128]
                        o2 = dst[:, hh * 256 + 128:hh * 256 + 256]
                        cosb = rsl[:, 0:128]
                        sinb = rsl[:, 128:256]
                        em.op(eng, lambda h, a=scr[:, 0:128], b=x1, c_=cosb:
                              h.tensor_tensor(a, b, c_, ALU.mult),
                              waits=tok_w + [t_rp[cs]])
                        em.op(eng, lambda h, a=scr[:, 128:256], b=x2, c_=sinb:
                              h.tensor_tensor(a, b, c_, ALU.mult))
                        em.op(eng, lambda h, a=o1, b=scr[:, 0:128],
                              c_=scr[:, 128:256]:
                              h.tensor_tensor(a, b, c_, ALU.subtract))
                        em.op(eng, lambda h, a=scr[:, 0:128], b=x2, c_=cosb:
                              h.tensor_tensor(a, b, c_, ALU.mult))
                        em.op(eng, lambda h, a=scr[:, 128:256], b=x1, c_=sinb:
                              h.tensor_tensor(a, b, c_, ALU.mult))
                        t_last = em.op(eng, lambda h, a=o2, b=scr[:, 0:128],
                                       c_=scr[:, 128:256]:
                                       h.tensor_tensor(a, b, c_, ALU.add),
                                       inc=inc)
                    return t_last

                t_qro = rope("vector", q_sb, q_ro, rsc,
                             [t_qkev, prev["t_qT"]], "dv")
                t_kro = rope("gpsimd", k_sb, k_ro, rscg,
                             [t_qkev, prev["t_kT"], prev["t_kd"]], "gp")
                prev["t_kro_dv"] = t_kro
                t_free_rope[cs] = (t_qro, t_kro)
                if _TAPS.get("q") == cs:
                    em.dma("scalar", "sts", taps["q"][:, :], q_ro[:, :],
                           waits=[t_qro])
                if _TAPS.get("k") == cs:
                    em.dma("scalar", "sts", taps["k"][:, :], k_ro[:, :],
                           waits=[t_kro])

                t_g = None
                for ic in range(8):
                    lh = hT[:, ic * 128:(ic + 1) * 128]
                    for nsl in range(4):
                        last = (ic == 7 and nsl == 3)
                        t_g = em.op(
                            "tensor",
                            lambda h, l=lh,
                            o=ps_proj[:, nsl * 512:(nsl + 1) * 512],
                            r=wg_sb[:, ic * 2 * D + nsl * 512:ic * 2 * D + nsl * 512 + 512],
                            st=(ic == 0), sp=(ic == 7):
                            h.matmul(o, l, r, start=st, stop=sp),
                            waits=[t_vev], inc="pe" if last else None)
                t_sg = em.op("scalar", lambda h: h.activation(
                    sg_bf[:, :], ps_proj[:, :], AF.Silu),
                    waits=[t_g, prev["t_z"]], inc="ac")
                prev["t_sg"] = t_sg
                if _TAPS.get("sig") == cs:
                    em.dma("scalar", "sts", taps["sig"][:, :], sg_bf[:, :],
                           waits=[t_sg])

                for j in range(8):
                    t_pe = em.op("tensor", lambda h, j=j: h.transpose(
                        ps_tr[:, j * 128:(j + 1) * 128],
                        q_ro[:, j * 128:(j + 1) * 128], ident[:, :]),
                        waits=[t_qro, t_ptr_free[0]],
                        inc="pe" if j == 7 else None)
                t_qT = em.op("scalar", lambda h: h.copy(qT[:, :], ps_tr[:, :]),
                             waits=[t_pe], inc="ac")
                for j in range(8):
                    t_pe = em.op("tensor", lambda h, j=j: h.transpose(
                        ps_tr[:, j * 128:(j + 1) * 128],
                        k_ro[:, j * 128:(j + 1) * 128], ident[:, :]),
                        waits=[t_kro, t_qT], inc="pe" if j == 7 else None)
                t_kT = em.op("scalar", lambda h: h.copy(kT[:, :], ps_tr[:, :]),
                             waits=[t_pe], inc="ac")
                t_ptr_free[0] = t_kT
                prev["t_qT"] = t_qT
                prev["t_kT"] = t_kT
                if _TAPS.get("qT") == cs:
                    em.dma("scalar", "sts", taps["qT"][:, :], qT[:, :],
                           waits=[t_qT])

                t_qdT = None
                for hh in range(H):
                    for dc in range(2):
                        j = 2 * hh + dc
                        t_qdT = em.op(
                            "vector",
                            lambda h, j=j, hh=hh: h.tensor_tensor(
                                qdT[:, j * 128:(j + 1) * 128],
                                qT[:, j * 128:(j + 1) * 128],
                                cross[:, hh * 128:(hh + 1) * 128], ALU.mult),
                            waits=[t_qT, t_cross, prev["t_upd_pe"]],
                            inc="dv" if (hh == H - 1 and dc == 1) else None)
                prev["t_qdT"] = t_qdT
                t_kd = None
                for hh in range(H):
                    t_kd = em.op("vector", lambda h, hh=hh: h.tensor_scalar(
                        kd_sb[:, hh * 256:(hh + 1) * 256],
                        k_ro[:, hh * 256:(hh + 1) * 256],
                        kdec[:, hh:hh + 1], None, ALU.mult),
                        waits=[t_kro, prev["t_upd_pe"]],
                        inc="dv" if hh == H - 1 else None)
                prev["t_kd"] = t_kd

                t_gss = []
                t_oev = {}
                upd_cols = [(0, 512), (1024, 1536)]
                t_upd_pe_last = None
                for hh in range(H):
                    t_scpe = None
                    for dc in range(2):
                        j = 2 * hh + dc
                        t_scpe = em.op("tensor", lambda h, j=j, st=(dc == 0),
                                       sp=(dc == 1): h.matmul(
                                           ps_sc[:, 0:128],
                                           kT[:, j * 128:(j + 1) * 128],
                                           qT[:, j * 128:(j + 1) * 128],
                                           start=st, stop=sp),
                                       waits=[t_kT, t_bank45],
                                       inc="pe" if dc == 1 else None)
                    t_scev = em.op("vector", lambda h, hh=hh: h.tensor_tensor(
                        sct[:, :], ps_sc[:, 0:128],
                        maskt[:, hh * 128:(hh + 1) * 128], ALU.mult),
                        waits=[t_scpe], inc="dv")
                    t_bank45 = t_scev
                    vsl = v_bf[:, hh * DV:(hh + 1) * DV]
                    em.op("tensor", lambda h, v_=vsl: h.matmul(
                        ps_o[:, :], sct[:, :], v_, start=True, stop=False),
                        waits=[t_scev, t_vev, t_o_free[0], t_sbf_tok[0],
                               t_qdT])
                    t_ope = None
                    for dc in range(2):
                        j = 2 * hh + dc
                        t_ope = em.op("tensor", lambda h, j=j, sp=(dc == 1):
                                      h.matmul(ps_o[:, :],
                                               qdT[:, j * 128:(j + 1) * 128],
                                               S_bf[:, j * DV:(j + 1) * DV],
                                               start=False, stop=sp),
                                      inc="pe" if dc == 1 else None)
                    t_q_ = em.op("scalar", lambda h, hh=hh: h.activation(
                        trash[:, 0:DV], ps_o[:, :], AF.Square,
                        accum_out=stats[:, 8 + hh:9 + hh]),
                        waits=[t_ope, prev["t_gn"]], inc="ac")
                    t_gss.append(t_q_)
                    t_oev[hh] = em.op("scalar", lambda h, hh=hh: h.copy(
                        o_sb[:, hh * DV:(hh + 1) * DV], ps_o[:, :]),
                        waits=[prev["t_z"]], inc="ac")
                    t_o_free[0] = t_oev[hh]
                    p0_, p1_ = upd_cols[hh % 2]
                    t_upe = {}
                    for dc in range(2):
                        pc = (p0_, p1_)[dc]
                        j = 2 * hh + dc
                        t_upe[dc] = em.op(
                            "tensor",
                            lambda h, j=j, pc=pc, v_=vsl: h.matmul(
                                ps_proj[:, pc:pc + 512],
                                kd_sb[:, j * 128:(j + 1) * 128], v_,
                                start=True, stop=True),
                            waits=[t_kd, t_sg, pair_tok[hh % 2]], inc="pe")
                    t_upd_pe_last = t_upe[1]
                    t_su = None
                    for dc in range(2):
                        pc = (p0_, p1_)[dc]
                        j = 2 * hh + dc
                        t_su = em.op(
                            "vector",
                            lambda h, j=j, pc=pc, g_=gCs[hh]:
                            h.scalar_tensor_tensor(
                                S_sb[:, j * DV:(j + 1) * DV],
                                S_sb[:, j * DV:(j + 1) * DV], g_,
                                ps_proj[:, pc:pc + 512], ALU.mult, ALU.add),
                            waits=[t_upe[dc], t_sbf_tok[0]],
                            inc="dv" if dc == 1 else None)
                    pair_tok[hh % 2] = t_su
                    t_sb_ = em.op("scalar", lambda h, hh=hh: h.copy(
                        S_bf[:, 2 * hh * DV:(2 * hh + 2) * DV],
                        S_sb[:, 2 * hh * DV:(2 * hh + 2) * DV]),
                        waits=[t_su], inc="ac")
                    if hh == H - 1:
                        t_sbf_tok[0] = t_sb_
                prev["t_upd_pe"] = t_upd_pe_last
                if _TAPS.get("S") == cs:
                    em.dma("scalar", "sts", taps["S"][:, :], S_sb[:, :],
                           waits=[t_sbf_tok[0]])
                if _TAPS.get("o") == cs:
                    em.dma("scalar", "sts", taps["o"][:, :], o_sb[:, :],
                           waits=[t_oev[H - 1]])

                em.opd("vector", lambda h: h.tensor_scalar(
                    hersc[:, 4:8], stats[:, 8:12], 1.0 / DV, EPS,
                    ALU.mult, ALU.add), waits=t_gss + [prev["t_gnr"]])
                t_gn = _heron(em, hersc[:, 4:8], hersc[:, 0:4], stats[:, 12:16],
                              stats[:, 16:20], 13, 1.0, 0.03)
                prev["t_gn"] = t_gn
                t_gnr = em.op("scalar", lambda h: h.copy(
                    stats[:, 20:24], stats[:, 16:20]), waits=[t_gn], inc="ac")
                prev["t_gnr"] = t_gnr

                t_z = None
                for hh in range(H):
                    t_z = em.op("vector", lambda h, hh=hh:
                                h.scalar_tensor_tensor(
                                    z_sb[:, hh * DV:(hh + 1) * DV],
                                    o_sb[:, hh * DV:(hh + 1) * DV],
                                    stats[:, 20 + hh:21 + hh],
                                    sg_bf[:, hh * DV:(hh + 1) * DV],
                                    ALU.mult, ALU.mult),
                                waits=[t_gnr, t_oev[hh], t_sg, t_zT_prev[0]],
                                inc="dv" if hh == H - 1 else None)
                prev["t_z"] = t_z
                if _TAPS.get("z") == cs:
                    em.dma("scalar", "sts", taps["z"][:, :], z_sb[:, :],
                           waits=[t_z])

                t_zhalf = []
                for half in range(2):
                    for j in range(8):
                        vj = half * 8 + j
                        t_pe = em.op("tensor", lambda h, vj=vj, j=j: h.transpose(
                            ps_tr[:, j * 128:(j + 1) * 128],
                            z_sb[:, vj * 128:(vj + 1) * 128], ident[:, :]),
                            waits=[t_z, t_ptr_free[0]],
                            inc="pe" if j == 7 else None)
                    t_ev = em.op("scalar", lambda h, half=half: h.copy(
                        zT_sb[:, half * 1024:(half + 1) * 1024], ps_tr[:, :]),
                        waits=[t_pe, t_st_z.get(cs - 1)], inc="ac")
                    t_ptr_free[0] = t_ev
                    t_zhalf.append(t_ev)
                t_zT_prev[0] = t_zhalf[-1]
                t_st_z[cs] = em.dma("scalar", "sts", zt_hbm[cs, :, :],
                                    zT_sb[:, :], waits=t_zhalf)

                prefetch(cs + 2)

            em.quiesce()
            em.replay_block()

        aw.close()
        if _MAX_PHASE < 2:
            return nc

        with ExitStack() as pb:
            sbb = lambda n, s, d=F32: pb.enter_context(nc.sbuf_tensor(n, s, d))
            wo_sb = sbb("b_wo", [128, 16 * D], BF16)
            xb = sbb("b_x", [128, 2 * D])
            ztb = sbb("b_zt", [128, 2 * 2 * D], BF16)
            tmpb = sbb("b_tmp", [128, D])
            xn2 = sbb("b_xn2", [128, D])
            h2 = sbb("b_h2", [128, D])
            hi32 = sbb("b_hi32", [128, D], I32)
            h2q = sbb("b_h2q", [128, D], BF16)
            h2qT = sbb("b_h2qT", [128, D], BF16)
            ps_wo = pb.enter_context(nc.psum_tensor("b_pswo", [128, 2 * D], F32))
            ps_tr = pb.enter_context(nc.psum_tensor("b_pstr", [128, D], BF16))

            modt = sbb("b_modt", [B, 6 * D], BF16)
            t_bcB = _bcast_tiles(em, nc, mod_hbm, modt, oneh, ps_wo[:, 0:1024],
                                 ad, [(0, 0, 2), (1, 1, 2), (2, 0, 4),
                                      (3, 1, 4), (4, 0, 3), (5, 1, 3)], t_oneh)
            if "adB" in taps:
                for i_ in range(6):
                    em.dma("scalar", "sts", taps["adB"][:, i_ * D:(i_ + 1) * D],
                           ad[i_][:, :], waits=[t_bcB])

            sap_ = bass.AP(wo_h, 0, [[D, 128], [128 * D, 16], [1, D]])
            dap_ = bass.AP(wo_sb, 0, [[16 * D, 128], [D, 16], [1, D]])
            t_wol = em.dma("gpsimd", "ldg", dap_, sap_)
            t_wo = None
            for vc in range(16):
                t_wo = em.op("vector", lambda h, vc=vc: h.tensor_scalar(
                    wo_sb[:, vc * D:(vc + 1) * D], wo_sb[:, vc * D:(vc + 1) * D],
                    gnw[:, vc:vc + 1], None, ALU.mult),
                    waits=[t_wol, t_gnw], inc="dv" if vc == 15 else None)

            if "wsum2" in taps:
                em.op("vector", lambda h: h.reduce_sum(
                    stats[:, 32:33], wo_sb[:, :], AX.X,
                    apply_absolute_value=True), waits=[t_wo], inc="dv")
                t_ws = None
                for i_ in range(6):
                    t_ws = em.op("vector", lambda h, i_=i_: h.reduce_sum(
                        stats[:, 34 + i_:35 + i_], ad[i_][:, :], AX.X,
                        apply_absolute_value=True), waits=[t_bcB], inc="dv")
                em.dma("scalar", "sts", taps["wsum2"][:, 0:1], stats[:, 32:33],
                       waits=[t_ws], slow=True)
                em.dma("scalar", "sts", taps["wsum2"][:, 2:8], stats[:, 34:40],
                       waits=[t_ws], slow=True)

            t_zt = {}
            t_x1in = {}
            t_free_x = {-2: None, -1: None}
            t_free_zt = {-2: None, -1: None}
            t_st_hq = {}
            t_st_x1 = {}
            t_wo_free = {-2: t_bcB, -1: t_bcB}
            t_ptr_free = [None]
            prevB = {"t_stats": None, "t_h2": None, "t_hqT": None,
                     "t_rel": None}

            def prefetchB(cs):
                if cs >= ncht:
                    return
                row = cs * 128
                t_zt[cs] = em.dma("sync", "lds",
                                  ztb[:, (cs % 2) * 2 * D:(cs % 2 + 1) * 2 * D],
                                  zt_hbm[cs, :, :],
                                  waits=[t_st_z[cs], t_free_zt[cs - 2]])
                t_x1in[cs] = em.dma("sync", "lds",
                                    xb[:, (cs % 2) * D:(cs % 2 + 1) * D],
                                    x_h[row:row + 128, :],
                                    waits=[t_free_x[cs - 2]])

            prefetchB(0)
            prefetchB(1)
            for cs in range(ncht):
                s, c = divmod(cs, nch)
                xsl = xb[:, (cs % 2) * D:(cs % 2 + 1) * D]
                ztsl = ztb[:, (cs % 2) * 2 * D:(cs % 2 + 1) * 2 * D]
                posl = ps_wo[:, (cs % 2) * D:(cs % 2 + 1) * D]
                t_pe = None
                for vc in range(16):
                    for nsl in range(2):
                        last = (vc == 15 and nsl == 1)
                        t_pe = em.op(
                            "tensor",
                            lambda h, l=ztsl[:, vc * 128:(vc + 1) * 128],
                            o=posl[:, nsl * 512:(nsl + 1) * 512],
                            r=wo_sb[:, vc * D + nsl * 512:vc * D + nsl * 512 + 512],
                            st=(vc == 0), sp=(vc == 15):
                            h.matmul(o, l, r, start=st, stop=sp),
                            waits=[t_zt[cs], t_wo, t_wo_free[cs - 2]],
                            inc="pe" if last else None)
                t_free_zt[cs] = t_pe
                em.op("vector", lambda h, a=ad[0 + s], p=posl: h.tensor_tensor(
                    tmpb[:, :], a[:, :], p, ALU.mult),
                    waits=[t_pe, t_x1in[cs], t_bcB])
                t_x1 = em.op("vector", lambda h, x=xsl: h.tensor_tensor(
                    x, x, tmpb[:, :], ALU.add), inc="dv")
                t_wo_free[cs] = t_x1
                if _TAPS.get("x1") == cs:
                    em.dma("scalar", "sts", taps["x1"][:, :], xsl, waits=[t_x1])

                em.op("scalar", lambda h, x=xsl: h.activation(
                    trash[:, :], x, AF.Identity, accum_out=stats[:, 0:1]),
                    waits=[t_x1, prevB["t_stats"]])
                t_ssq = em.op("scalar", lambda h, x=xsl: h.activation(
                    trash[:, :], x, AF.Square, accum_out=stats[:, 1:2]),
                    inc="ac")
                em.opd("vector", lambda h: h.tensor_single_scalar(
                    stats[:, 2:3], stats[:, 0:1], 1.0 / D, ALU.mult),
                    waits=[t_ssq])
                em.opd("vector", lambda h: h.tensor_single_scalar(
                    stats[:, 3:4], stats[:, 2:3], -1.0, ALU.mult))
                em.opd("vector", lambda h: h.tensor_tensor(
                    stats[:, 4:5], stats[:, 2:3], stats[:, 2:3], ALU.mult))
                em.opd("vector", lambda h: h.tensor_scalar(
                    hersc[:, 0:1], stats[:, 1:2], 1.0 / D, EPS, ALU.mult, ALU.add))
                em.opd("vector", lambda h: h.tensor_tensor(
                    hersc[:, 0:1], hersc[:, 0:1], stats[:, 4:5], ALU.subtract))
                _heron(em, hersc[:, 0:1], hersc[:, 1:2], hersc[:, 2:3],
                       stats[:, 5:6], 12, 0.5, 0.5)
                t_stats = em.opd("vector", lambda h: h.tensor_tensor(
                    stats[:, 6:7], stats[:, 3:4], stats[:, 5:6], ALU.mult),
                    inc="dv")
                prevB["t_stats"] = t_stats
                t_xn = em.op("scalar", lambda h, x=xsl: h.activation(
                    xn2[:, :], x, AF.Identity, bias=stats[:, 6:7],
                    scale=stats[:, 5:6]), waits=[t_stats, prevB["t_h2"]],
                    inc="ac")
                em.op("vector", lambda h, a=ad[2 + s]: h.tensor_tensor(
                    h2[:, :], xn2[:, :], a[:, :], ALU.mult), waits=[t_xn])
                t_h2 = em.op("vector", lambda h, a=ad[4 + s]: h.tensor_tensor(
                    h2[:, :], h2[:, :], a[:, :], ALU.add), inc="dv")
                prevB["t_h2"] = t_h2

                em.opd("vector", lambda h: h.reduce_max(
                    stats[:, 7:8], h2[:, :], AX.X, apply_absolute_value=True),
                    waits=[t_h2])
                em.opd("vector", lambda h: h.tensor_single_scalar(
                    stats[:, 7:8], stats[:, 7:8], 1e-5, ALU.max))
                em.opd("vector", lambda h: h.reciprocal(
                    stats[:, 24:25], stats[:, 7:8]), waits=[prevB["t_rel"]])
                em.opd("vector", lambda h: h.tensor_single_scalar(
                    stats[:, 24:25], stats[:, 24:25], 127.0, ALU.mult))
                t_qs = em.opd("vector", lambda h: h.reciprocal(
                    stats[:, 25:26], stats[:, 24:25]), inc="dv")
                t_rel = em.op("scalar", lambda h: h.copy(
                    stats[:, 26:28], stats[:, 24:26]), waits=[t_qs], inc="ac")
                prevB["t_rel"] = t_rel
                em.op("vector", lambda h: h.tensor_scalar(
                    hi32[:, :], h2[:, :], stats[:, 26:27], None, ALU.mult),
                    waits=[t_rel])
                t_h2q = em.op("vector", lambda h: h.tensor_scalar(
                    h2q[:, :], hi32[:, :], stats[:, 27:28], None, ALU.mult),
                    waits=[prevB["t_hqT"]], inc="dv")
                t_free_x[cs] = None
                if _TAPS.get("h2q") == cs:
                    em.dma("scalar", "sts", taps["h2q"][:, :], h2q[:, :],
                           waits=[t_h2q])

                for j in range(8):
                    t_pe = em.op("tensor", lambda h, j=j: h.transpose(
                        ps_tr[:, j * 128:(j + 1) * 128],
                        h2q[:, j * 128:(j + 1) * 128], ident[:, :]),
                        waits=[t_h2q, t_ptr_free[0]],
                        inc="pe" if j == 7 else None)
                t_hqT = em.op("scalar", lambda h: h.copy(h2qT[:, :], ps_tr[:, :]),
                              waits=[t_pe, t_st_hq.get(cs - 1)], inc="ac")
                prevB["t_hqT"] = t_hqT
                t_st_hq[cs] = em.dma("scalar", "sts", h2qt_hbm[cs, :, :],
                                     h2qT[:, :], waits=[t_hqT])
                t_ptr_free[0] = t_hqT
                t_st_x1[cs] = em.dma("scalar", "sts",
                                     x1_hbm[cs * 128:cs * 128 + 128, :], xsl,
                                     waits=[t_h2q])
                t_free_x[cs] = t_st_x1[cs]
                prefetchB(cs + 2)

            em.quiesce()
            em.replay_block()

        with ExitStack() as pc:
            sbc = lambda n, s, d=F32: pc.enter_context(nc.sbuf_tensor(n, s, d))
            wqg_sb = sbc("c_wqg", [128, 8 * INNER], BF16)
            wqu_sb = sbc("c_wqu", [128, 8 * INNER], BF16)
            hqb = sbc("c_hq", [128, 2 * D], BF16)
            sl_bf = sbc("c_sl", [128, 1024], BF16)
            u_bf = sbc("c_u", [128, INNER], BF16)
            ui32 = sbc("c_ui32", [128, 1024], I32)
            uq_sb = sbc("c_uq", [128, INNER], BF16)
            ps_g = pc.enter_context(nc.psum_tensor("c_psg", [128, 2 * D], F32))
            ps_u = pc.enter_context(nc.psum_tensor("c_psu", [128, 2 * D], F32))

            def wload_q(dst, src):
                sap_ = bass.AP(src, 0, [[INNER, 128], [128 * INNER, 8],
                                               [1, INNER]])
                dap_ = bass.AP(dst, 0, [[8 * INNER, 128], [INNER, 8],
                                               [1, INNER]])
                return em.dma("sync", "lds", dap_, sap_,
                              waits=[(("sts", em.cnt["sts"]))])
            t_wg = wload_q(wqg_sb, wqg_hbm)
            t_wu = wload_q(wqu_sb, wqu_hbm)
            if "scb" in taps:
                em.dma("scalar", "sts", taps["scb"][:, :], scb[:, :],
                       waits=[("ac", em.cnt["ac"])])

            t_hq = {}
            t_free_hq = {-2: None, -1: None}
            t_st_uq = {}
            t_uq_free = {-1: None}
            t_round_free = {}
            prevC = {"t_ur": None, "t_gp": None, "t_rel": None}

            def prefetchC(cs):
                if cs >= ncht:
                    return
                t_hq[cs] = em.dma("sync", "lds",
                                  hqb[:, (cs % 2) * D:(cs % 2 + 1) * D],
                                  h2qt_hbm[cs, :, :],
                                  waits=[t_st_hq[cs], t_free_hq[cs - 2]])

            prefetchC(0)
            prefetchC(1)
            for cs in range(ncht):
                hsl = hqb[:, (cs % 2) * D:(cs % 2 + 1) * D]
                t_urounds = []
                t_last_mm = None
                for r in range(4):
                    par = r % 2
                    pg = ps_g[:, par * D:(par + 1) * D]
                    pu = ps_u[:, par * D:(par + 1) * D]
                    for ic in range(8):
                        lh = hsl[:, ic * 128:(ic + 1) * 128]
                        for wsb, pp in ((wqg_sb, pg), (wqu_sb, pu)):
                            for nsl in range(2):
                                col = ic * INNER + r * D + nsl * 512
                                last = (ic == 7 and wsb is wqu_sb and nsl == 1)
                                t_last_mm = em.op(
                                    "tensor",
                                    lambda h, l=lh, o=pp[:, nsl * 512:(nsl + 1) * 512],
                                    r_=wsb[:, col:col + 512],
                                    st=(ic == 0), sp=(ic == 7):
                                    h.matmul(o, l, r_, start=st, stop=sp),
                                    waits=[t_hq[cs], t_wg, t_wu,
                                           t_round_free.get((cs, r - 2)),
                                           t_round_free.get((cs - 1, r + 2))],
                                    inc="pe" if last else None)
                    t_si = em.op("scalar", lambda h, pg=pg: h.activation(
                        sl_bf[:, :], pg, AF.Silu),
                        waits=[t_last_mm, prevC["t_ur"]], inc="ac")
                    t_ur = em.op("vector", lambda h, r=r, pu=pu: h.tensor_tensor(
                        u_bf[:, r * D:(r + 1) * D], sl_bf[:, :], pu, ALU.mult),
                        waits=[t_si, prevC["t_gp"]], inc="dv")
                    prevC["t_ur"] = t_ur
                    t_round_free[(cs, r)] = t_ur
                    t_urounds.append(t_ur)
                t_free_hq[cs] = t_last_mm
                if _TAPS.get("u") == cs:
                    em.dma("scalar", "sts", taps["u"][:, :], u_bf[:, :],
                           waits=t_urounds)

                em.opd("vector", lambda h: h.reduce_max(
                    stats[:, 7:8], u_bf[:, :], AX.X, apply_absolute_value=True),
                    waits=t_urounds)
                em.opd("vector", lambda h: h.tensor_single_scalar(
                    stats[:, 7:8], stats[:, 7:8], 1e-5, ALU.max))
                em.opd("vector", lambda h: h.reciprocal(
                    stats[:, 24:25], stats[:, 7:8]), waits=[prevC["t_rel"]])
                em.opd("vector", lambda h: h.tensor_single_scalar(
                    stats[:, 24:25], stats[:, 24:25], 127.0, ALU.mult))
                t_qs = em.opd("vector", lambda h: h.reciprocal(
                    stats[:, 25:26], stats[:, 24:25]), inc="dv")
                t_rel = em.op("scalar", lambda h: h.copy(
                    stats[:, 26:28], stats[:, 24:26]), waits=[t_qs], inc="ac")
                prevC["t_rel"] = t_rel
                t_q4 = []
                t_prev_f = None
                for quarter in range(4):
                    qs_ = slice(quarter * D, (quarter + 1) * D)
                    t_i = em.op("gpsimd", lambda h, q=qs_: h.tensor_scalar(
                        ui32[:, :], u_bf[:, q], stats[:, 26:27], None, ALU.mult),
                        waits=[t_rel, t_uq_free[cs - 1], t_prev_f], inc="gp")
                    t_f = em.op("vector", lambda h, q=qs_: h.tensor_scalar(
                        uq_sb[:, q], ui32[:, :], stats[:, 27:28], None, ALU.mult),
                        waits=[t_i], inc="dv")
                    t_prev_f = t_f
                    t_q4.append(t_f)
                prevC["t_gp"] = t_i
                t_st_uq[cs] = em.dma("scalar", "sts", uq_hbm[cs, :, :],
                                     uq_sb[:, :], waits=t_q4)
                t_uq_free[cs] = t_st_uq[cs]
                if _TAPS.get("uq") == cs:
                    em.dma("scalar", "sts", taps["uq"][:, :], uq_sb[:, :],
                           waits=t_q4)
                prefetchC(cs + 2)

            em.quiesce()
            em.replay_block()

        with ExitStack() as pd_:
            sbd = lambda n, s, d=F32: pd_.enter_context(nc.sbuf_tensor(n, s, d))
            wqd_sb = sbd("d_wqd", [128, 32 * D], BF16)
            uqb = sbd("d_uq", [128, 2 * INNER], BF16)
            uqT = sbd("d_uqT", [128, INNER], BF16)
            xb = sbd("d_x1", [128, 2 * D])
            tmpb = sbd("d_tmp", [128, D])
            ps_d = pd_.enter_context(nc.psum_tensor("d_psd", [128, 2 * D], F32))
            ps_tr = pd_.enter_context(nc.psum_tensor("d_pstr", [128, D], BF16))

            sap_ = bass.AP(wqd_hbm, 0, [[D, 128], [128 * D, 32], [1, D]])
            dap_ = bass.AP(wqd_sb, 0, [[32 * D, 128], [D, 32], [1, D]])
            t_wd = em.dma("sync", "lds", dap_, sap_,
                          waits=[("sts", em.cnt["sts"])])
            modt = sbd("d_modt", [B, 6 * D], BF16)
            t_bcD = _bcast_tiles(em, nc, mod_hbm, modt, oneh, ps_d[:, 0:1024],
                                 ad, [(0, 0, 5), (1, 1, 5)], t_oneh)
            if "adD" in taps:
                for i_ in range(2):
                    em.dma("scalar", "sts", taps["adD"][:, i_ * D:(i_ + 1) * D],
                           ad[i_][:, :], waits=[t_bcD])
            if "modb2" in taps:
                em.dma("scalar", "sts", taps["modb2"][:, :], modt[:, :],
                       waits=[t_bcD])

            t_uq = {}
            t_x1b = {}
            t_free_uq = {-2: None, -1: None}
            t_free_x = {-2: None, -1: None}
            t_ptr_free = [None]
            t_ps_free = {-2: t_bcD, -1: t_bcD}
            t_uqT_free = {-1: None}

            def prefetchD(cs):
                if cs >= ncht:
                    return
                t_uq[cs] = em.dma("sync", "lds",
                                  uqb[:, (cs % 2) * INNER:(cs % 2 + 1) * INNER],
                                  uq_hbm[cs, :, :],
                                  waits=[t_st_uq[cs], t_free_uq[cs - 2]])
                t_x1b[cs] = em.dma("sync", "lds",
                                   xb[:, (cs % 2) * D:(cs % 2 + 1) * D],
                                   x1_hbm[cs * 128:cs * 128 + 128, :],
                                   waits=[t_st_x1[cs], t_free_x[cs - 2]])

            prefetchD(0)
            prefetchD(1)
            t_y_last = None
            for cs in range(ncht):
                s, c = divmod(cs, nch)
                usl = uqb[:, (cs % 2) * INNER:(cs % 2 + 1) * INNER]
                xsl = xb[:, (cs % 2) * D:(cs % 2 + 1) * D]
                t_uqT = None
                for half in range(4):
                    for j in range(8):
                        mj = half * 8 + j
                        t_pe = em.op("tensor", lambda h, j=j,
                                     u_=usl[:, mj * 128:(mj + 1) * 128]:
                                     h.transpose(ps_tr[:, j * 128:(j + 1) * 128],
                                                 u_, ident[:, :]),
                                     waits=[t_uq[cs], t_ptr_free[0],
                                            t_uqT_free[cs - 1]],
                                     inc="pe" if j == 7 else None)
                    t_uqT = em.op("scalar", lambda h, half=half: h.copy(
                        uqT[:, half * D:(half + 1) * D], ps_tr[:, :]),
                        waits=[t_pe], inc="ac")
                    t_ptr_free[0] = t_uqT
                pdsl = ps_d[:, (cs % 2) * D:(cs % 2 + 1) * D]
                t_pe = None
                for mc in range(32):
                    for nsl in range(2):
                        last = (mc == 31 and nsl == 1)
                        t_pe = em.op(
                            "tensor",
                            lambda h, l=uqT[:, mc * 128:(mc + 1) * 128],
                            o=pdsl[:, nsl * 512:(nsl + 1) * 512],
                            r=wqd_sb[:, mc * D + nsl * 512:mc * D + nsl * 512 + 512],
                            st=(mc == 0), sp=(mc == 31):
                            h.matmul(o, l, r, start=st, stop=sp),
                            waits=[t_uqT, t_wd, t_ps_free[cs - 2]],
                            inc="pe" if last else None)
                t_uqT_free[cs] = t_pe
                t_free_uq[cs] = t_pe
                em.op("vector", lambda h, a=ad[0 + s], p=pdsl: h.tensor_tensor(
                    tmpb[:, :], a[:, :], p, ALU.mult),
                    waits=[t_pe, t_x1b[cs], t_bcD])
                t_y = em.op("vector", lambda h, x=xsl: h.tensor_tensor(
                    x, x, tmpb[:, :], ALU.add), inc="dv")
                t_ps_free[cs] = t_y
                t_y_last = em.dma("scalar", "sts",
                                  y_h[cs * 128:cs * 128 + 128, :], xsl,
                                  waits=[t_y])
                t_free_x[cs] = t_y_last
                prefetchD(cs + 2)

            em.quiesce()
            em.replay_block()

        for s_ in ("lds", "ldg", "sts", "pe", "dv", "ac", "gp"):
            em.op("sync", lambda h, ss=s_: h.sem_clear(em.sems[ss]))
        em.replay_block()

    return nc


_NC_CACHE = {}


def _host_tables():
    t = np.arange(T, dtype=np.float64)
    inv = 1.0 / (10000.0 ** (np.arange(0, DK, 2, dtype=np.float64) / DK))
    fr = t[:, None] * inv[None, :]
    cos = np.cos(fr).astype(np.float32)
    sin = np.sin(fr).astype(np.float32)
    ident = np.eye(128, dtype=ml_dtypes.bfloat16)
    maskt = np.zeros((128, 4 * 128), np.float32)
    crossp = np.zeros((128, 4 * 128), np.float32)
    kdecp = np.zeros((128, H), np.float32)
    idx = np.arange(C, dtype=np.float64)
    for hh in range(H):
        g = 1.0 - 2.0 ** (-5.0 - hh)
        diff = idx[None, :] - idx[:, None]
        m = np.where(diff >= 0, g ** diff, 0.0) * (DK ** -0.5)
        maskt[:, hh * 128:(hh + 1) * 128] = m.astype(np.float32)
        crossp[:, hh * 128:(hh + 1) * 128] = np.broadcast_to(
            (g ** (idx + 1.0)) * (DK ** -0.5), (128, 128)).astype(np.float32)
        kdecp[:, hh] = (g ** (C - 1.0 - idx)).astype(np.float32)
    return cos, sin, ident, maskt, crossp, kdecp


def kernel(L0_attn_x, L0_attn_Wq, L0_attn_Wk, L0_attn_Wv, L0_attn_Wg,
           L0_attn_Wo, L0_attn_gnorm_w, L0_ffn_w_gate, L0_ffn_w_up,
           L0_ffn_w_down, L0_c, L0_wmod):
    if "nc" not in _NC_CACHE:
        _NC_CACHE["nc"] = build_program()
    nc = _NC_CACHE["nc"]

    cos, sin, ident, maskt, crossp, kdecp = _host_tables()
    c_full = np.asarray(L0_c, np.float32)
    onehot = np.zeros((B, 2 * 128), ml_dtypes.bfloat16)
    for b_ in range(2):
        onehot[b_, b_ * 128:(b_ + 1) * 128] = 1
    gnw_pack = np.empty((128, 16), np.float32)
    gnw_flat = np.asarray(L0_attn_gnorm_w, np.float32)
    for vc in range(16):
        gnw_pack[:, vc] = gnw_flat[(np.arange(128) + 128 * vc) % DV]

    common = {
        "wq_t": np.ascontiguousarray(np.asarray(L0_attn_Wq, np.float32).T),
        "wk_t": np.ascontiguousarray(np.asarray(L0_attn_Wk, np.float32).T),
        "wv_t": np.ascontiguousarray(np.asarray(L0_attn_Wv, np.float32).T),
        "wg_t": np.ascontiguousarray(np.asarray(L0_attn_Wg, np.float32).T),
        "wo_t": np.ascontiguousarray(np.asarray(L0_attn_Wo, np.float32).T),
        "wmod": np.ascontiguousarray(np.asarray(L0_wmod, np.float32)),
        "wgate_t": np.ascontiguousarray(np.asarray(L0_ffn_w_gate, np.float32).T),
        "wup_t": np.ascontiguousarray(np.asarray(L0_ffn_w_up, np.float32).T),
        "wdown_t": np.ascontiguousarray(np.asarray(L0_ffn_w_down, np.float32).T),
        "gnw_pack": gnw_pack, "ident": ident,
        "ident_f32": np.eye(128, dtype=np.float32),
        "rope_cos": cos, "rope_sin": sin, "maskt_pack": maskt,
        "cross_pack": crossp, "kdec_pack": kdecp,
        "onehot": onehot,
    }
    x = np.asarray(L0_attn_x, np.float32).reshape(B, T, D)
    in_maps = []
    for i in range(NCORES):
        m = dict(common)
        m["x"] = np.ascontiguousarray(
            x[SEQ_PER_CORE * i:SEQ_PER_CORE * (i + 1)].reshape(TOK, D))
        cpad = np.zeros((D, 128), np.float32)
        cpad[:, 0:SEQ_PER_CORE] = c_full[SEQ_PER_CORE * i:SEQ_PER_CORE * (i + 1)].T
        m["c_t"] = cpad
        in_maps.append(m)

    res = run_bass_kernel_spmd(nc, in_maps, core_ids=list(range(NCORES)))
    out = np.concatenate([r["y"].reshape(SEQ_PER_CORE, T, D)
                          for r in res.results], axis=0)
    return np.ascontiguousarray(out.astype(np.float32))
